# version 01 (final)
# baseline (speedup 1.0000x reference)
import numpy as np
import ml_dtypes
from contextlib import ExitStack

import concourse.bass as bass
import concourse.mybir as mybir
from concourse.bass_utils import run_bass_kernel_spmd

F32 = mybir.dt.float32
BF16 = mybir.dt.bfloat16
AF = mybir.ActivationFunctionType
ALU = mybir.AluOpType
AX = mybir.AxisListType

B, T, D, H, DH = 2, 8192, 1024, 16, 64
NOWN = 2048
E = 8
EPS = 1e-5
NEG = -1e30

STOP_AFTER = None
DEBUG_SCRATCH_OUT = False

ENGS = ("sync", "vector", "tensor", "scalar", "gpsimd")


class Sched:
    def __init__(self):
        self.ops = []
        self.cnt = {}

    def op(self, eng, fn, waits=(), inc=None, inck=1):
        ws = []
        for w in waits:
            if w is None:
                continue
            s, v = w
            if v > 0:
                ws.append((s, v))
        self.ops.append((eng, fn, ws, inc, inck))
        if inc is not None:
            self.cnt[inc] = self.cnt.get(inc, 0) + inck
            return (inc, self.cnt[inc])
        return None

    def cur(self, sem):
        return (sem, self.cnt.get(sem, 0))

    def all_tokens(self):
        return [(s, v) for s, v in self.cnt.items()]


def build_nc():
    nc = bass.Bass("TRN2", target_bir_lowering=False)
    scr_kind = "ExternalOutput" if DEBUG_SCRATCH_OUT else "Internal"

    def inp(name, shape):
        return nc.dram_tensor(name, list(shape), F32, kind="ExternalInput").ap()

    xt = inp("xt", (D, T))
    wq = inp("wq", (128, 8192))
    wk = inp("wk", (128, 8192))
    wv = inp("wv", (128, 8192))
    wo = inp("wo", (128, 8192))
    w1 = inp("w1", (128, 32768))
    w2 = inp("w2", (128, 32768))
    gw = inp("gw", (E, 128, 16384))
    uw = inp("uw", (E, 128, 16384))
    dw = inp("dw", (E, 128, 16384))
    rw = inp("rw", (128, 64))
    dbm = inp("dbm", (8, 1024))
    bq = inp("bq", (128, 8))
    bk = inp("bk", (128, 8))
    bvr = inp("bvr", (1, 1024))
    bo = inp("bo", (128, 8))
    b1 = inp("b1", (128, 32))
    b2 = inp("b2", (128, 8))
    g1 = inp("g1", (128, 8))
    be1 = inp("be1", (128, 8))
    g2 = inp("g2", (128, 8))
    be2 = inp("be2", (128, 8))
    gbi = inp("gbi", (128, 128))
    ubi = inp("ubi", (128, 128))
    dsc = inp("dsc", (1, 1))
    tv4 = inp("tv4", (4, 2048))
    msk4 = nc.dram_tensor("msk4", [4, 1], mybir.dt.uint8, kind="ExternalInput").ap()
    ind4 = nc.dram_tensor("ind4", [4, 8192], BF16, kind="ExternalInput").ap()
    p1d = inp("p1d", (128, 128))
    p0d = inp("p0d", (128, 1024))
    w2i = inp("w2i", (128, 1920))

    out_t = nc.dram_tensor("out_t", [D, NOWN], F32, kind="ExternalOutput").ap()

    kscr = nc.dram_tensor("kscr", [8 * 128 * T], BF16, kind=scr_kind).ap()
    vscr = nc.dram_tensor("vscr", [H * 128 * 4160], BF16, kind=scr_kind).ap()
    qscr = nc.dram_tensor("qscr", [8 * 128 * NOWN], BF16, kind=scr_kind).ap()
    xnscr = nc.dram_tensor("xnscr", [128 * 8 * NOWN], F32, kind=scr_kind).ap()
    gscr = nc.dram_tensor("gscr", [128 * 8 * NOWN], F32, kind=scr_kind).ap()
    wtscr = nc.dram_tensor("wtscr", [8 * NOWN], F32, kind=scr_kind).ap()
    ydbg = nc.dram_tensor("ydbg", [128 * 8 * NOWN], F32,
                          kind=scr_kind).ap() if DEBUG_SCRATCH_OUT else None
    rdbg = nc.dram_tensor("rdbg", [128, 32], F32,
                          kind=scr_kind).ap() if DEBUG_SCRATCH_OUT else None
    adbg = nc.dram_tensor("adbg", [128, 16384], BF16,
                          kind=scr_kind).ap() if DEBUG_SCRATCH_OUT else None
    x1dbg = nc.dram_tensor("x1dbg", [128 * 8 * NOWN], F32,
                           kind=scr_kind).ap() if DEBUG_SCRATCH_OUT else None
    sdbg = nc.dram_tensor("sdbg", [128 * 8 * NOWN], F32,
                          kind=scr_kind).ap() if DEBUG_SCRATCH_OUT else None
    odbg = nc.dram_tensor("odbg", [128 * 8 * NOWN], F32,
                          kind=scr_kind).ap() if DEBUG_SCRATCH_OUT else None
    p2dbg = nc.dram_tensor("p2dbg", [128, 4096], F32,
                           kind=scr_kind).ap() if DEBUG_SCRATCH_OUT else None
    p2bdbg = nc.dram_tensor("p2bdbg", [128, 8192], BF16,
                            kind=scr_kind).ap() if DEBUG_SCRATCH_OUT else None

    def dap(t, off, pat):
        return bass.AP(t.tensor, off, pat)

    S = Sched()
    es = ExitStack()
    with es:
        sem_names = ["s_pe", "s_act", "s_dve", "s_gpc",
                     "d_x0", "d_x1", "d_wl", "d_kst0", "d_kst1", "d_vst0", "d_vst1",
                     "d_xnst", "d_qst", "d_at0", "d_at1", "d_mv", "d_aug",
                     "d_xn0", "d_xn1", "d_w4a0", "d_w4a1", "d_w4b0", "d_w4b1",
                     "d_w50", "d_w51", "d_wer", "d_wts", "d_g0", "d_g1",
                     "d_6g0", "d_6g1", "d_6x0", "d_6x1", "d_out", "d_misc"]
        sems = {n: es.enter_context(nc.semaphore(n)) for n in sem_names}

        def sb(stack, name, shape, dt, side=None):
            kw = {"side": side} if side else {}
            return stack.enter_context(nc.sbuf_tensor(name, list(shape), dt, **kw))

        def psn(stack, name, shape):
            return stack.enter_context(nc.psum_tensor(name, list(shape), F32))

        def dma(q, dst, src, waits=(), sem="d_misc"):
            return S.op(q, lambda e, d=dst, s=src: e.dma_start(out=d, in_=s),
                        waits=waits, inc=sem, inck=16)

        def barrier():
            S.op("vector", lambda e: e.drain(), inc="s_dve")
            S.op("scalar", lambda e: e.drain(), inc="s_act")
            S.op("gpsimd", lambda e: e.drain(), inc="s_gpc")
            S.op("tensor", lambda e: e.drain(), inc="s_pe")
            toks = S.all_tokens()
            for eng in ENGS:
                S.op(eng, None, waits=toks)

        def drain(eng):
            sem = {"vector": "s_dve", "scalar": "s_act", "gpsimd": "s_gpc"}[eng]
            return S.op(eng, lambda e: e.drain(), inc=sem)

        padsink = sb(es, "padsink", (128, 4), F32)

        def pad(eng, rd_ap):
            drain(eng)
            np_ = rd_ap.shape[0]
            if eng == "vector":
                S.op(eng, lambda e, a=rd_ap, np_=np_: e.tensor_reduce(
                    padsink[0:np_, 0:1], a, AX.X, ALU.max))
            else:
                S.op(eng, lambda e, a=rd_ap, np_=np_: e.activation(
                    padsink[0:np_, 0:1], a, AF.Identity, bias=0.0, scale=1.0))
            return drain(eng)

        xnown_b = sb(es, "xnown_b", (128, 8 * NOWN), BF16)
        ones_f = sb(es, "ones_f", (128, 128), F32)
        ones_b = sb(es, "ones_b", (128, 1), BF16)
        bq8 = sb(es, "bq8", (128, 8), F32)
        bk_sb = sb(es, "bk_sb", (128, 8), F32)
        bo_sb = sb(es, "bo_sb", (128, 8), F32)
        b2_sb = sb(es, "b2_sb", (128, 8), F32)
        g1_sb = sb(es, "g1_sb", (128, 8), F32)
        be1_sb = sb(es, "be1_sb", (128, 8), F32)
        g2_sb = sb(es, "g2_sb", (128, 8), F32)
        be2_sb = sb(es, "be2_sb", (128, 8), F32)
        dsp = sb(es, "dsp", (128, 2), F32)
        tvhi = sb(es, "tvhi", (4, 2048), BF16)
        strow = sb(es, "strow", (1, 2048), F32)
        stmp = sb(es, "stmp", (1, 512), F32)
        epsv = sb(es, "epsv", (1, 1), F32)

        es0 = ExitStack()
        with es0:
            dsc_sb = sb(es0, "dsc_sb", (1, 1), F32)
            tv4_sb = sb(es0, "tv4_sb", (4, 2048), F32)
            msk_sb = sb(es0, "msk_sb", (4, 1), mybir.dt.uint8)
            twf = sb(es0, "twf", (4, 2048), F32)
            hib = sb(es0, "hib", (4, 2048), BF16)
            lob = sb(es0, "lob", (4, 2048), BF16)
            hif = sb(es0, "hif", (4, 2048), F32)
            ps0 = psn(es0, "ps0", (128, 512))

            for dst, src in [
                (bq8[:, :], bq), (bk_sb[:, :], bk), (bo_sb[:, :], bo),
                (b2_sb[:, :], b2), (g1_sb[:, :], g1), (be1_sb[:, :], be1),
                (g2_sb[:, :], g2), (be2_sb[:, :], be2), (dsc_sb[:, :], dsc),
                (tv4_sb[:, :], tv4), (msk_sb[:, :], msk4),
            ]:
                dma("sync", dst, src)
            t_ld0 = S.cur("d_misc")

            S.op("vector", lambda e: e.memset(ones_f[:, :], 1.0), waits=[t_ld0])
            S.op("vector", lambda e: e.memset(epsv[0:1, :], EPS))
            t_on = S.op("vector", lambda e: e.memset(ones_b[:, :], 1.0), inc="s_dve")
            t_pe0 = S.op("tensor", lambda e: e.matmul(ps0[:, 0:1], ones_f[0:1, :],
                                                      dsc_sb[0:1, 0:1], start=True, stop=True),
                         waits=[t_on], inc="s_pe")
            t = S.op("vector", lambda e: e.tensor_scalar(dsp[:, 0:1], ps0[:, 0:1],
                                                         0.125, None, ALU.mult),
                     waits=[t_pe0], inc="s_dve")
            t = S.op("vector", lambda e: e.tensor_scalar(dsp[:, 1:2], dsp[:, 0:1], -2.0,
                                                         None, ALU.mult),
                     waits=[t], inc="s_dve")
            t = S.op("vector", lambda e: e.tensor_scalar(twf[0:4, :], tv4_sb[0:4, :],
                                                         dsp[0:4, 1:2], None, ALU.mult),
                     waits=[t], inc="s_dve")
            t = S.op("vector", lambda e: e.tensor_copy(hib[0:4, :], twf[0:4, :]),
                     waits=[t], inc="s_dve")
            t = S.op("vector", lambda e: e.tensor_copy(hif[0:4, :], hib[0:4, :]),
                     waits=[t], inc="s_dve")
            t = S.op("vector", lambda e: e.tensor_sub(hif[0:4, :], twf[0:4, :],
                                                      hif[0:4, :]),
                     waits=[t], inc="s_dve")
            t = S.op("vector", lambda e: e.tensor_copy(lob[0:4, :], hif[0:4, :]),
                     waits=[t], inc="s_dve")
            mbc = bass.AP(msk_sb, 0, [[1, 4], [0, 2048]])
            t = S.op("vector", lambda e: e.select(tvhi[0:4, :], mbc, hib[0:4, :],
                                                  lob[0:4, :]),
                     waits=[t], inc="s_dve")
            t = S.op("vector", lambda e: e.tensor_scalar(bq8[:, :], bq8[:, :], 0.125,
                                                         None, ALU.mult),
                     waits=[t], inc="s_dve")
            barrier()

        es1 = ExitStack()
        with es1:
            wstg = sb(es1, "wstg", (128, 1024), F32)
            bvbc = sb(es1, "bvbc", (128, 1024), F32)
            xw_f = [sb(es1, f"xw_f{i}", (128, 4096), F32) for i in range(2)]
            xw_b = [sb(es1, f"xw_b{i}", (128, 4096), BF16) for i in range(2)]
            xsq_b = [sb(es1, f"xsq_b{i}", (128, 4096), BF16) for i in range(2)]
            tmp32 = sb(es1, "tmp32", (128, 4096), F32)
            vgrp = [sb(es1, f"vgrp{i}", (128, 4160), BF16) for i in range(2)]
            ps_s1 = psn(es1, "ps_s1", (1, 512))
            ps_s2 = psn(es1, "ps_s2", (1, 512))
            ps_mu = psn(es1, "ps_mu", (128, 512))
            ps_rs = psn(es1, "ps_rs", (128, 512))
            ps_k = [psn(es1, f"ps_k{i}", (128, 512)) for i in range(2)]
            ps_v = [psn(es1, f"ps_v{i}", (128, 512)) for i in range(2)]
            kstg = xsq_b

            es1a = ExitStack()
            with es1a:
                wk_b = sb(es1a, "wk_b", (128, 8192), BF16)
                wv_b = sb(es1a, "wv_b", (128, 8192), BF16)
                t_wc = None
                for wb, wsrc in ((wk_b, wk), (wv_b, wv)):
                    for ch in range(8):
                        t_l = dma("gpsimd", wstg[:, :],
                                  dap(wsrc, ch * 1024, [[8192, 128], [1, 1024]]),
                                  waits=[t_wc] if t_wc else [], sem="d_wl")
                        t_wc = S.op("gpsimd", lambda e, d=wb, ch=ch: e.tensor_copy(
                            d[:, ch * 1024:(ch + 1) * 1024], wstg[:, :]),
                            waits=[t_l], inc="s_gpc")
                t_wcast = t_wc
                t_lb = dma("sync", bvbc[0:1, 0:1024], bvr, sem="d_misc")
                t_b1 = S.op("tensor", lambda e: e.matmul(ps_mu[:, :], ones_f[0:1, :],
                                                         bvbc[0:1, 0:512],
                                                         start=True, stop=True),
                            waits=[t_lb], inc="s_pe")
                t_b3 = S.op("tensor", lambda e: e.matmul(ps_rs[:, :], ones_f[0:1, :],
                                                         bvbc[0:1, 512:1024],
                                                         start=True, stop=True),
                            inc="s_pe")
                t_b2 = S.op("vector", lambda e: e.tensor_copy(bvbc[:, 0:512], ps_mu[:, :]),
                            waits=[t_b3], inc="s_dve")
                t_bvbc = S.op("vector", lambda e: e.tensor_copy(bvbc[:, 512:1024],
                                                                ps_rs[:, :]), inc="s_dve")
                t_von = None
                for i in range(2):
                    a = bass.AP(vgrp[i], 64, [[4160, 128], [65, 64], [1, 1]])
                    t_von = S.op("vector", lambda e, a=a: e.memset(a, 1.0), inc="s_dve")

                tk_xw = [None, None]
                tk_xwb_pe = [None, None]
                tk_stats = None
                tk_bc = t_bvbc
                tk_kp = [None, None]
                tk_vp = [None, None]
                tk_kstg = [None, None]
                tk_vgrp = [t_von, t_von]

                for tw in range(16):
                    p = tw % 2
                    own = tw < 4
                    src_ = dap(xt, tw * 512, [[T, 128], [128 * T, 8], [1, 512]])
                    t_ld = dma("sync", xw_f[p][:, :], src_, waits=[tk_xw[p]],
                               sem=f"d_x{p}")
                    t_cast = S.op("vector", lambda e, p=p: e.tensor_copy(
                        xw_b[p][:, :], xw_f[p][:, :]),
                        waits=[t_ld, tk_xwb_pe[p]], inc="s_dve")
                    t_sq = S.op("vector", lambda e, p=p: e.tensor_mul(
                        xsq_b[p][:, :], xw_b[p][:, :], xw_b[p][:, :]),
                        waits=[tk_kstg[p]], inc="s_dve")
                    for c in range(8):
                        S.op("tensor", lambda e, p=p, c=c: e.matmul(
                            ps_s1[0:1, :], ones_b[:, :],
                            xw_b[p][:, c * 512:(c + 1) * 512],
                            start=(c == 0), stop=(c == 7)),
                            waits=[t_cast, tk_stats])
                    t_pst = None
                    for c in range(8):
                        t_pst = S.op("tensor", lambda e, p=p, c=c: e.matmul(
                            ps_s2[0:1, :], ones_b[:, :],
                            xsq_b[p][:, c * 512:(c + 1) * 512],
                            start=(c == 0), stop=(c == 7)),
                            waits=[t_sq, tk_stats], inc=("s_pe" if c == 7 else None))
                    t_sa = S.op("vector", lambda e: e.tensor_scalar(
                        strow[0:1, 0:512], ps_s1[0:1, :], 1.0 / D, None, ALU.mult),
                        waits=[t_pst], inc="s_dve")
                    t_sb_ = S.op("vector", lambda e: e.tensor_scalar(
                        strow[0:1, 512:1024], ps_s2[0:1, :], 1.0 / D, None, ALU.mult),
                        waits=[t_sa], inc="s_dve")
                    t_sc_ = S.op("vector", lambda e: e.tensor_mul(
                        stmp[0:1, :], strow[0:1, 0:512], strow[0:1, 0:512]),
                        waits=[t_sb_], inc="s_dve")
                    t_var = S.op("vector", lambda e: e.tensor_sub(
                        strow[0:1, 1024:1536], strow[0:1, 512:1024], stmp[0:1, :]),
                        waits=[t_sc_], inc="s_dve")
                    tk_stats = t_var
                    S.op("scalar", lambda e: e.activation(
                        strow[0:1, 1536:2048], strow[0:1, 1024:1536], AF.Ln,
                        bias=epsv[0:1, 0:1], scale=1.0), waits=[t_var])
                    t_rstd = S.op("scalar", lambda e: e.activation(
                        strow[0:1, 512:1024], strow[0:1, 1536:2048], AF.Exp,
                        bias=0.0, scale=-0.5), inc="s_act")
                    S.op("tensor", lambda e: e.matmul(
                        ps_mu[:, :], ones_f[0:1, :], strow[0:1, 0:512],
                        start=True, stop=True), waits=[t_rstd, tk_bc])
                    t_pbc = S.op("tensor", lambda e: e.matmul(
                        ps_rs[:, :], ones_f[0:1, :], strow[0:1, 512:1024],
                        start=True, stop=True), inc="s_pe")
                    mu_b = bass.AP(ps_mu, 0, [[512, 128], [0, 8], [1, 512]])
                    rs_b = bass.AP(ps_rs, 0, [[512, 128], [0, 8], [1, 512]])
                    x4 = bass.AP(xw_f[p], 0, [[4096, 128], [512, 8], [1, 512]])
                    tm4 = bass.AP(tmp32, 0, [[4096, 128], [512, 8], [1, 512]])
                    S.op("vector", lambda e, a=tm4, b=x4, m=mu_b: e.tensor_tensor(
                        a, b, m, ALU.subtract), waits=[t_pbc])
                    if own:
                        xb4 = bass.AP(xnown_b, tw * 512, [[16384, 128], [2048, 8], [1, 512]])
                    else:
                        xb4 = bass.AP(xw_b[p], 0, [[4096, 128], [512, 8], [1, 512]])
                    t_xnb = S.op("vector", lambda e, a=xb4, b=tm4, r=rs_b: e.tensor_tensor(
                        a, b, r, ALU.mult), waits=[t_pst], inc="s_dve")
                    tk_bc = t_xnb
                    tk_xw[p] = t_xnb
                    if own:
                        xo4 = bass.AP(xw_f[p], 0, [[4096, 128], [512, 8], [1, 512]])
                        t_xo = S.op("vector", lambda e, a=xo4, b=tm4, r=rs_b:
                                    e.tensor_tensor(a, b, r, ALU.mult),
                                    inc="s_dve")
                        tk_bc = t_xo
                        xsc = dap(xnscr, tw * 512, [[16384, 128], [2048, 8], [1, 512]])
                        t_dx = drain("vector")
                        tk_xnfst = dma("gpsimd", xsc, xo4, waits=[t_dx],
                                       sem="d_xnst")
                        tk_xw[p] = tk_xnfst

                    def xnb_rhs(dc, off, width, tw=tw, p=p, own=own):
                        if own:
                            return xnown_b[:, dc * 2048 + tw * 512 + off:
                                           dc * 2048 + tw * 512 + off + width]
                        return xw_b[p][:, dc * 512 + off: dc * 512 + off + width]

                    t_ke = None
                    for pc in range(8):
                        kb = pc % 2
                        t_pk = None
                        for dc in range(8):
                            t_pk = S.op("tensor", lambda e, pc=pc, dc=dc, kb=kb,
                                        r=xnb_rhs(dc, 0, 512): e.matmul(
                                ps_k[kb][:, :],
                                wk_b[:, pc * 1024 + dc * 128: pc * 1024 + (dc + 1) * 128],
                                r, start=(dc == 0), stop=(dc == 7)),
                                waits=[t_xnb, tk_kp[kb],
                                       S.cur("s_gpc") if tw == 0 else None],
                                inc=("s_pe" if dc == 7 else None))
                        t_ke = S.op("vector", lambda e, pc=pc, p=p, kb=kb: e.tensor_scalar(
                            kstg[p][:, pc * 512:(pc + 1) * 512], ps_k[kb][:, :],
                            bk_sb[:, pc:pc + 1], None, ALU.add),
                            waits=[t_pk], inc="s_dve")
                        tk_kp[kb] = t_ke
                    dstk = dap(kscr, tw * 512, [[T, 128], [128 * T, 8], [1, 512]])
                    t_dk = drain("vector")
                    tk_kstg[p] = dma("gpsimd", dstk,
                                     bass.AP(kstg[p], 0, [[4096, 128], [512, 8], [1, 512]]),
                                     waits=[t_dk], sem=f"d_kst{p}")
                    t_ve = None
                    for g in range(8):
                        ktl, half = g // 2, g % 2
                        vb = g % 2
                        t_pv = None
                        for dc in range(8):
                            rhs = bass.AP(wv_b, half * 4096 + dc * 128,
                                          [[8192, 128], [1024, 4], [1, 128]])
                            t_pv = S.op("tensor", lambda e, r=rhs, vb=vb,
                                        l=xnb_rhs(dc, ktl * 128, 128), dc=dc: e.matmul(
                                ps_v[vb][:, :], l, r,
                                start=(dc == 0), stop=(dc == 7)),
                                waits=[t_xnb, tk_vp[vb]],
                                inc=("s_pe" if dc == 7 else None))
                        vdst = bass.AP(vgrp[p], half * 8 * 260 + ktl * 65,
                                       [[4160, 128], [260, 8], [1, 64]])
                        t_ve = S.op("vector", lambda e, d=vdst, vb=vb, hf=half:
                                    e.tensor_tensor(d, ps_v[vb][:, :],
                                                    bvbc[:, hf * 512:(hf + 1) * 512],
                                                    ALU.add),
                                    waits=[t_pv, tk_vgrp[p]], inc="s_dve")
                        tk_vp[vb] = t_ve
                    dstv = dap(vscr, tw * 260, [[4160, 128], [128 * 4160, 16], [1, 260]])
                    t_dv = drain("vector")
                    tk_vgrp[p] = dma("gpsimd", dstv,
                                     bass.AP(vgrp[p], 0, [[4160, 128], [260, 16], [1, 260]]),
                                     waits=[t_dv], sem=f"d_vst{p}")
                    tk_xwb_pe[p] = t_pst if own else t_pv
            es1b = ExitStack()
            with es1b:
                wq_b = sb(es1b, "wq_b", (128, 8192), BF16)
                qstg = sb(es1b, "qstg", (128, 4096), BF16)
                t_wcq = None
                for ch in range(8):
                    t_l = dma("gpsimd", wstg[:, :],
                              dap(wq, ch * 1024, [[8192, 128], [1, 1024]]),
                              waits=[S.cur("s_gpc")] if ch == 0 else [t_wcq],
                              sem="d_wl")
                    t_wcq = S.op("gpsimd", lambda e, ch=ch: e.tensor_copy(
                        wq_b[:, ch * 1024:(ch + 1) * 1024], wstg[:, :]),
                        waits=[t_l], inc="s_gpc")
                tk_qst = None
                for q4 in range(4):
                    t_qe = None
                    for pc in range(8):
                        kb = pc % 2
                        t_pq = None
                        for dc in range(8):
                            t_pq = S.op("tensor", lambda e, pc=pc, dc=dc, kb=kb, q4=q4:
                                        e.matmul(
                                ps_k[kb][:, :],
                                wq_b[:, pc * 1024 + dc * 128: pc * 1024 + (dc + 1) * 128],
                                xnown_b[:, dc * 2048 + q4 * 512:
                                        dc * 2048 + (q4 + 1) * 512],
                                start=(dc == 0), stop=(dc == 7)),
                                waits=[t_wcq, tk_kp[kb]],
                                inc=("s_pe" if dc == 7 else None))
                        t_qe = S.op("vector", lambda e, pc=pc, kb=kb, q4=q4: e.tensor_scalar(
                            qstg[:, pc * 512:(pc + 1) * 512], ps_k[kb][:, :],
                            0.125, bq8[:, pc:pc + 1], ALU.mult, ALU.add),
                            waits=[t_pq, tk_qst], inc="s_dve")
                        tk_kp[kb] = t_qe
                    dstq = dap(qscr, q4 * 512, [[NOWN, 128], [128 * NOWN, 8], [1, 512]])
                    t_dq = drain("vector")
                    tk_qst = dma("gpsimd", dstq,
                                 bass.AP(qstg, 0, [[4096, 128], [512, 8], [1, 512]]),
                                 waits=[t_dq], sem="d_qst")
            barrier()
        if STOP_AFTER == "P1":
            return _emit(nc, S, sems)

        es5 = ExitStack()
        with es5:
            hbuf = sb(es5, "hbuf", (128, 8 * NOWN), BF16)
            y_f = sb(es5, "y_f", (128, 8 * NOWN), F32)
            wsf = [sb(es5, f"wsf{i}", (128, 1024), F32) for i in range(2)]
            wgb = [sb(es5, f"wgb{i}", (128, 1024), BF16) for i in range(2)]
            wub = [sb(es5, f"wub{i}", (128, 1024), BF16) for i in range(2)]
            wdb = [sb(es5, f"wdb{i}", (128, 1024), BF16) for i in range(2)]
            sgt = sb(es5, "sgt", (128, 512), BF16)
            tub = sb(es5, "tub", (128, 512), BF16)
            rtokf = sb(es5, "rtokf", (128, 8), F32)
            m1c = sb(es5, "m1c", (128, 4), F32)
            elr = sb(es5, "elr", (128, 8), F32)
            wtok = sb(es5, "wtok", (128, 8), F32)
            wT_stg = sb(es5, "wT_stg", (8, 128), F32)
            werow = sb(es5, "werow", (1, 512), F32)
            wbc_b = sb(es5, "wbc_b", (128, 2048), BF16)
            gout = [sb(es5, f"gout{i}", (128, 2048), F32) for i in range(2)]
            identf = sb(es5, "identf", (128, 128), F32)
            jj_sb = sb(es5, "jj_sb", (128, 128), F32)
            pidx5 = sb(es5, "pidx5", (128, 1), F32)
            rwb = sb(es5, "rwb", (128, 64), BF16)
            rw_f = sb(es5, "rw_f", (128, 64), F32)
            dbm_sb = sb(es5, "dbm_sb", (8, 1024), BF16)
            dbm_f = sb(es5, "dbm_f", (8, 256), F32)
            gb_sb = sb(es5, "gb_sb", (128, 128), F32)
            ub_sb = sb(es5, "ub_sb", (128, 128), F32)
            wT_b = sb(es5, "wT_b", (8, 2048), BF16)
            ps_r = psn(es5, "ps_r", (128, 8))
            ps_t8 = psn(es5, "ps_t8", (8, 128))
            ps_g = [psn(es5, f"ps_g{i}", (128, 512)) for i in range(2)]
            ps_y = [psn(es5, f"ps_y{i}", (128, 512)) for i in range(2)]

            t_l5 = None
            for dst, src_ in [(rw_f[:, :], rw),
                              (gb_sb[:, :], gbi), (ub_sb[:, :], ubi)]:
                t_l5 = dma("sync", dst, src_, sem="d_misc")
            t_rwb = S.op("vector", lambda e: e.tensor_copy(rwb[:, :], rw_f[:, :]),
                         waits=[t_l5], inc="s_dve")
            t_dbc = None
            for ch in range(4):
                t_ldb = dma("sync", dbm_f[:, :],
                            dap(dbm, ch * 256, [[1024, 8], [1, 256]]),
                            waits=[t_dbc] if t_dbc else [], sem="d_wl")
                t_dbc = S.op("vector", lambda e, ch=ch: e.tensor_copy(
                    dbm_sb[0:8, ch * 256:(ch + 1) * 256], dbm_f[0:8, :]),
                    waits=[t_ldb], inc="s_dve")
            t_gp5 = S.op("gpsimd", lambda e: e.iota(jj_sb[:, :], [[1, 128]],
                                                    channel_multiplier=0,
                                                    allow_small_or_imprecise_dtypes=True),
                         inc="s_gpc")
            t_gp6 = S.op("gpsimd", lambda e: e.iota(pidx5[:, 0:1], [[0, 1]],
                                                    channel_multiplier=1,
                                                    allow_small_or_imprecise_dtypes=True),
                         inc="s_gpc")
            t_id5 = S.op("vector", lambda e: e.tensor_scalar(identf[:, :], jj_sb[:, :],
                                                             pidx5[:, 0:1], None,
                                                             ALU.is_equal),
                         waits=[t_gp5, t_gp6], inc="s_dve")
            tk_r = None
            tk_wtok_pe = None
            t_wTb = None
            t_wts = None
            for tile in range(16):
                t_pr = None
                for dc in range(8):
                    t_pr = S.op("tensor", lambda e, tile=tile, dc=dc: e.matmul(
                        ps_r[:, :],
                        xnown_b[:, dc * 2048 + tile * 128: dc * 2048 + (tile + 1) * 128],
                        rwb[:, dc * 8:(dc + 1) * 8],
                        start=(dc == 0), stop=(dc == 7)),
                        waits=[tk_r, t_rwb], inc=("s_pe" if dc == 7 else None))
                t = S.op("vector", lambda e: e.tensor_copy(rtokf[:, :], ps_r[:, :]),
                         waits=[t_pr], inc="s_dve")
                t = S.op("vector", lambda e: e.tensor_reduce(m1c[:, 0:1], rtokf[:, :],
                                                             AX.X, ALU.max),
                         waits=[t], inc="s_dve")
                t = S.op("vector", lambda e: e.tensor_scalar(m1c[:, 1:2], m1c[:, 0:1],
                                                             -1.0, None, ALU.mult),
                         waits=[t], inc="s_dve")
                t = S.op("vector", lambda e: e.tensor_scalar(elr[:, :], rtokf[:, :],
                                                             m1c[:, 0:1], None,
                                                             ALU.is_equal),
                         waits=[t], inc="s_dve")
                t = S.op("vector", lambda e: e.scalar_tensor_tensor(
                    elr[:, :], elr[:, :], NEG, rtokf[:, :], ALU.mult, ALU.add),
                    waits=[t], inc="s_dve")
                t = S.op("vector", lambda e: e.tensor_reduce(m1c[:, 2:3], elr[:, :],
                                                             AX.X, ALU.max),
                         waits=[t], inc="s_dve")
                t_keep = S.op("vector", lambda e: e.tensor_scalar(
                    wtok[:, :], rtokf[:, :], m1c[:, 2:3], None, ALU.is_ge),
                    waits=[t], inc="s_dve")
                tk_r = t_keep
                t_el = S.op("scalar", lambda e: e.activation(elr[:, :], rtokf[:, :],
                                                             AF.Exp, bias=m1c[:, 1:2],
                                                             scale=1.0),
                            waits=[t_keep], inc="s_act")
                t = S.op("vector", lambda e: e.tensor_mul(wtok[:, :], wtok[:, :],
                                                          elr[:, :]),
                         waits=[t_el], inc="s_dve")
                t = S.op("vector", lambda e: e.tensor_reduce(m1c[:, 3:4], wtok[:, :],
                                                             AX.X, ALU.add),
                         waits=[t], inc="s_dve")
                t = S.op("vector", lambda e: e.reciprocal(m1c[:, 0:1], m1c[:, 3:4]),
                         waits=[t], inc="s_dve")
                t_wt = S.op("vector", lambda e: e.tensor_scalar(wtok[:, :], wtok[:, :],
                                                                m1c[:, 0:1], None,
                                                                ALU.mult),
                            waits=[t, tk_wtok_pe], inc="s_dve")
                if DEBUG_SCRATCH_OUT and tile == 0:
                    t_dbg = S.op("vector", lambda e: e.tensor_copy(
                        wbc_b[:, 0:8], rtokf[:, :]), inc="s_dve")
                    t_dbg2 = S.op("vector", lambda e: e.tensor_copy(
                        wbc_b[:, 8:16], wtok[:, :]), inc="s_dve")
                    dma("gpsimd", rdbg[:, 0:8], wbc_b[:, 0:8], waits=[t_dbg])
                    dma("gpsimd", rdbg[:, 8:16], wbc_b[:, 8:16], waits=[t_dbg2])
                t_tr = S.op("tensor", lambda e: e.transpose(ps_t8[:, :], wtok[:, :],
                                                            identf[:, :]),
                            waits=[t_wt, t_id5], inc="s_pe")
                tk_wtok_pe = t_tr
                t_ws = S.op("vector", lambda e: e.tensor_copy(
                    wT_stg[0:8, :], ps_t8[0:8, :]),
                    waits=[t_tr, t_wts], inc="s_dve")
                t_wTb = S.op("vector", lambda e, tile=tile: e.tensor_copy(
                    wT_b[0:8, tile * 128:(tile + 1) * 128], ps_t8[0:8, :]), inc="s_dve")
                t_dw5 = drain("vector")
                t_wts = dma("gpsimd", dap(wtscr, tile * 128, [[2048, 8], [1, 128]]),
                            wT_stg[0:8, :], waits=[t_dw5], sem="d_wts")

            tk_hb_pe = None
            tk_wsf = [None, None]
            tk_wgb_pe = [None, None]
            tk_wub_pe = [None, None]
            tk_wdb_pe = [None, None]
            tk_psg = [None, None]
            tk_psy = [None, None]
            tk_sgt = None
            tk_tub = None
            tk_werow_pe = None
            for e_i in range(E):
                t_wbc = None
                for q4 in range(4):
                    t_wr = dma("sync", werow[0:1, :],
                               dap(wtscr, e_i * 2048 + q4 * 512, [[512, 1], [1, 512]]),
                               waits=[t_wts, tk_werow_pe], sem="d_wer")
                    t_pw = S.op("tensor", lambda e, q4=q4: e.matmul(
                        ps_g[0][:, :], ones_f[0:1, :], werow[0:1, 0:512],
                        start=True, stop=True),
                        waits=[t_wr, tk_psg[0]], inc="s_pe")
                    tk_werow_pe = t_pw
                    t_wbc = S.op("vector", lambda e, q4=q4: e.tensor_copy(
                        wbc_b[:, q4 * 512:(q4 + 1) * 512], ps_g[0][:, :]),
                        waits=[t_pw, tk_hb_pe], inc="s_dve")
                    tk_psg[0] = t_wbc
                for fg in range(2):
                    t_hb = None
                    t_pg = t_pu = None
                    for f8 in range(8):
                        ffc = fg * 8 + f8
                        wi = f8 % 2
                        t_lg = dma("sync", wsf[wi][:, :],
                                   dap(gw, e_i * 128 * 16384 + ffc * 1024,
                                       [[16384, 128], [1, 1024]]),
                                   waits=[tk_wsf[wi]], sem=f"d_w5{wi}")
                        t_cg = S.op("gpsimd", lambda e, wi=wi: e.tensor_copy(
                            wgb[wi][:, :], wsf[wi][:, :]),
                            waits=[t_lg, tk_wgb_pe[wi]], inc="s_gpc")
                        wi2 = 1 - wi
                        t_lu = dma("sync", wsf[wi2][:, :],
                                   dap(uw, e_i * 128 * 16384 + ffc * 1024,
                                       [[16384, 128], [1, 1024]]),
                                   waits=[tk_wsf[wi2]], sem=f"d_w5{wi2}")
                        t_cu = S.op("gpsimd", lambda e, wi=wi, wi2=wi2: e.tensor_copy(
                            wub[wi][:, :], wsf[wi2][:, :]),
                            waits=[t_lu, tk_wub_pe[wi]], inc="s_gpc")
                        tk_wsf[wi] = t_cg
                        tk_wsf[wi2] = t_cu
                        for w4 in range(4):
                            for dc in range(8):
                                t_pg = S.op("tensor", lambda e, wi=wi, dc=dc, w4=w4: e.matmul(
                                    ps_g[0][:, :], wgb[wi][:, dc * 128:(dc + 1) * 128],
                                    xnown_b[:, dc * 2048 + w4 * 512:
                                            dc * 2048 + (w4 + 1) * 512],
                                    start=(dc == 0), stop=(dc == 7)),
                                    waits=[t_cg, tk_psg[0]],
                                    inc=("s_pe" if dc == 7 else None))
                            t_sg = S.op("scalar", lambda e, ffc=ffc, e_i=e_i: e.activation(
                                sgt[:, :], ps_g[0][:, :], AF.Silu,
                                bias=gb_sb[:, e_i * 16 + ffc: e_i * 16 + ffc + 1],
                                scale=1.0),
                                waits=[t_pg, tk_sgt, t_l5], inc="s_act")
                            tk_psg[0] = t_sg
                            for dc in range(8):
                                t_pu = S.op("tensor", lambda e, wi=wi, dc=dc, w4=w4: e.matmul(
                                    ps_g[1][:, :], wub[wi][:, dc * 128:(dc + 1) * 128],
                                    xnown_b[:, dc * 2048 + w4 * 512:
                                            dc * 2048 + (w4 + 1) * 512],
                                    start=(dc == 0), stop=(dc == 7)),
                                    waits=[t_cu, tk_psg[1]],
                                    inc=("s_pe" if dc == 7 else None))
                            t_tu = S.op("vector",
                                        lambda e, ffc=ffc, e_i=e_i: e.scalar_tensor_tensor(
                                            tub[:, :], ps_g[1][:, :],
                                            ub_sb[:, e_i * 16 + ffc: e_i * 16 + ffc + 1],
                                            sgt[:, :], ALU.add, ALU.mult),
                                        waits=[t_pu, t_sg, tk_tub, t_l5], inc="s_dve")
                            tk_psg[1] = t_tu
                            t_dtu = drain("vector")
                            t_hb = S.op("vector", lambda e, f8=f8, w4=w4: e.tensor_tensor(
                                hbuf[:, f8 * 2048 + w4 * 512: f8 * 2048 + (w4 + 1) * 512],
                                tub[:, :], wbc_b[:, w4 * 512:(w4 + 1) * 512], ALU.mult),
                                waits=[tk_hb_pe, t_wbc, t_dtu], inc="s_dve")
                            tk_tub = t_hb
                            tk_sgt = t_hb
                        tk_wgb_pe[wi] = t_pg
                        tk_wub_pe[wi] = t_pu
                    t_hlast = t_hb
                    t_pd = None
                    for dc in range(8):
                        wi = dc % 2
                        t_ldw = dma("sync", wsf[wi][:, :],
                                    dap(dw, e_i * 128 * 16384 + dc * 2048 + fg * 1024,
                                        [[16384, 128], [1, 1024]]),
                                    waits=[tk_wsf[wi]], sem=f"d_w5{wi}")
                        t_cd = S.op("gpsimd", lambda e, wi=wi: e.tensor_copy(
                            wdb[wi][:, :], wsf[wi][:, :]),
                            waits=[t_ldw, tk_wdb_pe[wi]], inc="s_gpc")
                        tk_wsf[wi] = t_cd
                        for w4 in range(4):
                            yb = w4 % 2
                            for f8 in range(8):
                                t_pd = S.op("tensor", lambda e, wi=wi, f8=f8, w4=w4, yb=yb:
                                            e.matmul(
                                                ps_y[yb][:, :],
                                                wdb[wi][:, f8 * 128:(f8 + 1) * 128],
                                                hbuf[:, f8 * 2048 + w4 * 512:
                                                     f8 * 2048 + (w4 + 1) * 512],
                                                start=(f8 == 0), stop=(f8 == 7)),
                                            waits=[t_cd, t_hlast, tk_psy[yb]],
                                            inc=("s_pe" if f8 == 7 else None))
                            ysl = y_f[:, dc * 2048 + w4 * 512: dc * 2048 + (w4 + 1) * 512]
                            if e_i == 0 and fg == 0:
                                t_ye = S.op("vector", lambda e, ysl=ysl, yb=yb: e.tensor_copy(
                                    ysl, ps_y[yb][:, :]), waits=[t_pd], inc="s_dve")
                            else:
                                t_ye = S.op("vector",
                                            lambda e, ysl=ysl, yb=yb: e.tensor_tensor(
                                                ysl, ps_y[yb][:, :], ysl, ALU.add),
                                            waits=[t_pd], inc="s_dve")
                            tk_psy[yb] = t_ye
                        tk_wdb_pe[wi] = t_pd
                    tk_hb_pe = t_pd
            tk_go = [None, None]
            for dc in range(8):
                gb_ = dc % 2
                t_si = None
                for w4 in range(4):
                    yb = w4 % 2
                    t_pb = S.op("tensor", lambda e, dc=dc, w4=w4, yb=yb: e.matmul(
                        ps_y[yb][:, :], dbm_sb[0:8, dc * 128:(dc + 1) * 128],
                        wT_b[0:8, w4 * 512:(w4 + 1) * 512], start=True, stop=True),
                        waits=[tk_psy[yb], t_dbc], inc="s_pe")
                    ysl = y_f[:, dc * 2048 + w4 * 512: dc * 2048 + (w4 + 1) * 512]
                    t_ye = S.op("vector", lambda e, ysl=ysl, yb=yb: e.tensor_tensor(
                        ysl, ps_y[yb][:, :], ysl, ALU.add), waits=[t_pb], inc="s_dve")
                    tk_psy[yb] = t_ye
                    t_si = S.op("scalar", lambda e, ysl=ysl, gb_=gb_, w4=w4: e.activation(
                        gout[gb_][:, w4 * 512:(w4 + 1) * 512], ysl, AF.Silu,
                        bias=0.0, scale=1.0),
                        waits=[t_ye, tk_go[gb_]], inc="s_act")
                gdst = dap(gscr, dc * 2048, [[16384, 128], [1, 2048]])
                t_ds5 = drain("scalar")
                tk_go[gb_] = dma("gpsimd", gdst, gout[gb_][:, :], waits=[t_ds5],
                                 sem=f"d_g{gb_}")
            if DEBUG_SCRATCH_OUT:
                dma("gpsimd", dap(ydbg, 0, [[16384, 128], [1, 16384]]), y_f[:, :],
                    waits=[S.cur("s_dve")], sem="d_misc")
            barrier()
        if STOP_AFTER == "P5":
            return _emit(nc, S, sems)

        es23 = ExitStack()
        abuf = es23.enter_context(nc.sbuf_tensor("abuf", [128, 16384], BF16, side="right"))
        es2 = ExitStack()
        with es2:
            kbuf = [sb(es2, f"kbuf{i}", (128, 8192), BF16) for i in range(2)]
            vbuf = [sb(es2, f"vbuf{i}", (128, 4160), BF16) for i in range(2)]
            qbuf = [sb(es2, f"qbuf{i}", (128, 2048), BF16) for i in range(2)]
            pbuf = [sb(es2, f"pbuf{i}", (128, 1024), BF16) for i in range(2)]
            tmpd = [sb(es2, f"tmpd{i}", (128, 1024), F32) for i in range(2)]
            tmpa = sb(es2, "tmpa", (128, 2048), BF16)
            rlrow = sb(es2, "rlrow", (128, 1024), F32)
            rbf = sb(es2, "rbf", (128, 1024), F32)
            ball = sb(es2, "ball", (128, 2048), F32)
            w2s = sb(es2, "w2s", (128, 1920), F32)
            ps_sS = [es2.enter_context(nc.psum_tensor(f"ps_sS{i}", [128, 1024], F32))
                     for i in range(2)]
            ps_A = es2.enter_context(nc.psum_tensor("ps_A", [128, 1024], F32))
            ps_B = es2.enter_context(nc.psum_tensor("ps_B", [128, 1024], F32))

            es2i = ExitStack()
            with es2i:
                p1_sb = sb(es2i, "p1_sb", (128, 128), F32)
                p0_sb = sb(es2i, "p0_sb", (128, 1024), F32)
                t_l1 = dma("sync", p1_sb[:, :], p1d, sem="d_aug")
                t_l2 = dma("sync", p0_sb[:, :], p0d, sem="d_aug")
                t_l3 = dma("sync", w2s[:, :], w2i, sem="d_aug")
                t_ball = None
                for h in range(16):
                    p1b = bass.AP(p1_sb, 0, [[128, 128], [64, 2], [1, 64]])
                    p0b = bass.AP(p0_sb, h * 64, [[1024, 128], [0, 2], [1, 64]])
                    ballv = bass.AP(ball, h * 128, [[2048, 128], [64, 2], [1, 64]])
                    t_ball = S.op("vector", lambda e, a=ballv, b=p1b, c=p0b:
                                  e.scalar_tensor_tensor(a, b, dsp[:, 0:1], c,
                                                         ALU.mult, ALU.add),
                                  waits=[t_l1, t_l2], inc="s_dve")
                S.op("vector", lambda e: e.tensor_scalar(w2s[:, :], w2s[:, :],
                                                         dsp[:, 0:1], None, ALU.mult),
                     waits=[t_l3])
                t_w2s = S.op("vector", lambda e: e.tensor_scalar(w2s[:, :], w2s[:, :],
                                                                 -1.0, None, ALU.mult),
                             inc="s_dve")
                for i in range(2):
                    S.op("vector", lambda e, i=i: e.memset(kbuf[i][64:128, :], 0.0))
                    S.op("vector", lambda e, i=i: e.memset(qbuf[i][64:128, :], 0.0),
                         inc="s_dve")
                t_ms = pad("vector", qbuf[1][64:128, 0:512])
                t_aug2 = None
                for i in range(2):
                    dma("gpsimd", kbuf[i][64:68, :], ind4, waits=[t_ms], sem="d_aug")
                    t_aug2 = dma("gpsimd", qbuf[i][64:68, :], tvhi[0:4, :], waits=[t_ms],
                                 sem="d_aug")

            def h_loads(h2, waits):
                i = h2 % 2
                pr, hh = h2 // 2, h2 % 2
                dma("sync", kbuf[i][0:64, :],
                    dap(kscr, pr * 128 * T + hh * 64 * T, [[T, 64], [1, 8192]]),
                    waits=waits, sem=f"d_at{i}")
                dma("sync", vbuf[i][:, :],
                    dap(vscr, h2 * 128 * 4160, [[4160, 128], [1, 4160]]),
                    sem=f"d_at{i}")
                return dma("sync", qbuf[i][0:64, :],
                           dap(qscr, pr * 128 * NOWN + hh * 64 * NOWN,
                               [[NOWN, 64], [1, 2048]]), sem=f"d_at{i}")

            tk_bufs = [None, None]
            tk_ld2 = [None, None]
            tk_psS = [[t_aug2], [t_aug2]]
            tk_pb = [None, None]
            tk_tmpd = [None, None]
            tk_psA = t_aug2
            tk_psB = None
            tk_tmpa = None
            tk_ab = None
            tk_ld2[0] = h_loads(0, [t_ball, t_w2s, t_aug2])
            for h2 in range(16):
                i = h2 % 2
                pr, hh = h2 // 2, h2 % 2
                if h2 + 1 < 16:
                    tk_ld2[(h2 + 1) % 2] = h_loads(h2 + 1, [tk_bufs[(h2 + 1) % 2]])
                t_pv = None
                for w in range(2):
                    def s_mm(kt, w=w, i=i):
                        sp = ps_sS[kt % 2]
                        S.op("tensor", lambda e, kt=kt, sp=sp, w=w, i=i: e.matmul(
                            sp[:, 0:512], kbuf[i][:, kt * 128:(kt + 1) * 128],
                            qbuf[i][:, w * 1024: w * 1024 + 512], start=True, stop=True),
                            waits=[tk_ld2[i]] + tk_psS[kt % 2])
                        return S.op("tensor", lambda e, kt=kt, sp=sp, w=w, i=i: e.matmul(
                            sp[:, 512:1024], kbuf[i][:, kt * 128:(kt + 1) * 128],
                            qbuf[i][:, w * 1024 + 512: w * 1024 + 1024],
                            start=True, stop=True), inc="s_pe")

                    t_s2s = [s_mm(0)]
                    for kt in range(64):
                        sp = ps_sS[kt % 2]
                        diag = (8 * w <= kt < 8 * w + 8)
                        if kt + 1 < 64:
                            t_s2s.append(s_mm(kt + 1))
                        t_s2 = t_s2s[kt]
                        cidx = h2 * 128 + w * 64 + kt
                        if diag:
                            k = kt - 8 * w
                            t_d = S.op("vector", lambda e, kt=kt, k=k, sp=sp: e.tensor_tensor(
                                tmpd[kt % 2][:, :], sp[:, :],
                                w2s[:, 896 - 128 * k: 1920 - 128 * k], ALU.add),
                                waits=[t_s2, tk_tmpd[kt % 2]], inc="s_dve")
                            tk_psS[kt % 2] = [t_d]
                            t_x = S.op("scalar", lambda e, kt=kt, cidx=cidx: e.activation(
                                pbuf[kt % 2][:, :], tmpd[kt % 2][:, :], AF.Exp,
                                bias=ball[:, cidx:cidx + 1], scale=1.0),
                                waits=[t_d, tk_pb[kt % 2]], inc="s_act")
                            tk_tmpd[kt % 2] = t_x
                        else:
                            t_x = S.op("scalar",
                                       lambda e, kt=kt, cidx=cidx, sp=sp: e.activation(
                                           pbuf[kt % 2][:, :], sp[:, :], AF.Exp,
                                           bias=ball[:, cidx:cidx + 1], scale=1.0),
                                       waits=[t_s2, tk_pb[kt % 2]], inc="s_act")
                            tk_psS[kt % 2] = [t_x]
                        S.op("tensor", lambda e, i=i, kt=kt: e.matmul(
                            ps_A[0:65, 0:512], vbuf[i][:, kt * 65:(kt + 1) * 65],
                            pbuf[kt % 2][:, 0:512], start=(kt == 0), stop=(kt == 63)),
                            waits=[t_x] + ([tk_psA] if kt == 0 else []))
                        t_pv = S.op("tensor", lambda e, i=i, kt=kt: e.matmul(
                            ps_A[0:65, 512:1024], vbuf[i][:, kt * 65:(kt + 1) * 65],
                            pbuf[kt % 2][:, 512:1024], start=(kt == 0), stop=(kt == 63)),
                            inc="s_pe")
                        tk_pb[kt % 2] = t_pv
                    t_gd = S.op("tensor", lambda e: e.matmul(
                        ps_B[0:1, 0:1], ones_f[0:1, 0:1], ones_f[0:1, 0:1],
                        start=True, stop=True), inc="s_pe")
                    t_rc = S.op("vector", lambda e: e.reciprocal(rlrow[64:65, :],
                                                                 ps_A[64:65, :]),
                                waits=[t_gd], inc="s_dve")
                    S.op("tensor", lambda e: e.matmul(
                        ps_B[0:64, 0:512], ones_f[64:65, 0:64], rlrow[64:65, 0:512],
                        start=True, stop=True), waits=[t_rc, tk_psB])
                    t_pb2 = S.op("tensor", lambda e: e.matmul(
                        ps_B[0:64, 512:1024], ones_f[64:65, 0:64], rlrow[64:65, 512:1024],
                        start=True, stop=True), inc="s_pe")
                    t_rb = S.op("vector", lambda e: e.tensor_copy(rbf[0:64, :],
                                                                  ps_B[0:64, :]),
                                waits=[t_pb2], inc="s_dve")
                    tk_psB = t_rb
                    if hh == 0:
                        dsta = abuf[0:64, pr * 2048 + w * 1024: pr * 2048 + (w + 1) * 1024]
                    else:
                        dsta = tmpa[0:64, w * 1024:(w + 1) * 1024]
                    t_fin = S.op("vector", lambda e, d=dsta: e.tensor_tensor(
                        d, ps_A[0:64, :], rbf[0:64, :], ALU.mult),
                        waits=[tk_tmpa if hh == 1 else None, t_rb], inc="s_dve")
                    tk_psA = t_fin
                    tk_ab = t_fin
                tk_bufs[i] = t_pv
                if hh == 1:
                    t_da2 = pad("vector", tmpa[0:64, 1536:2048])
                    tk_tmpa = dma("gpsimd", abuf[64:128, pr * 2048:(pr + 1) * 2048],
                                  tmpa[0:64, :], waits=[t_da2], sem="d_mv")
            if DEBUG_SCRATCH_OUT:
                dma("gpsimd", adbg[:, :], abuf[:, :],
                    waits=[S.cur("s_dve")], sem="d_misc")
                dma("sync", dap(p2dbg, 0, [[4096, 128], [1, 2048]]), ball[:, :],
                    sem="d_misc")
                dma("sync", dap(p2dbg, 2048, [[4096, 128], [1, 1920]]), w2s[:, :],
                    sem="d_misc")
                dma("sync", dap(p2bdbg, 0, [[8192, 128], [1, 2048]]),
                    qbuf[0][:, :], sem="d_misc")
                dma("sync", dap(p2bdbg, 2048, [[8192, 128], [1, 2048]]),
                    qbuf[1][:, :], sem="d_misc")
                dma("sync", dap(p2bdbg, 4096, [[8192, 128], [1, 2048]]),
                    kbuf[0][:, 0:2048], sem="d_misc")
                dma("sync", dap(p2bdbg, 6144, [[8192, 128], [1, 2048]]),
                    kbuf[1][:, 0:2048], sem="d_misc")
            barrier()
        if STOP_AFTER == "P2":
            return _emit(nc, S, sems)

        es34 = ExitStack()
        r1 = es34.enter_context(nc.sbuf_tensor("r1", [128, 8 * NOWN], F32))
        esx1 = ExitStack()
        x1b = esx1.enter_context(nc.sbuf_tensor("x1b", [128, 8 * NOWN], BF16))
        es3 = ExitStack()
        with es3:
            wof = sb(es3, "wof", (128, 1024), F32)
            wob = sb(es3, "wob", (128, 8192), BF16)
            xnt = [sb(es3, f"xnt{i}", (128, 512), F32) for i in range(2)]
            c1b = sb(es3, "c1b", (128, 512), BF16)
            c2b = sb(es3, "c2b", (128, 512), BF16)
            t1f = sb(es3, "t1f", (128, 512), F32)
            t2f = sb(es3, "t2f", (128, 512), F32)
            ps_w = [psn(es3, f"ps_w{i}", (128, 512)) for i in range(2)]
            ps_s1b = psn(es3, "ps_s1b", (1, 512))
            ps_s2b = psn(es3, "ps_s2b", (1, 512))
            ps_mu2 = psn(es3, "ps_mu2", (128, 512))
            ps_rs2 = psn(es3, "ps_rs2", (128, 512))

            t_cw = None
            for ch in range(8):
                t_lw = dma("gpsimd", wof[:, :],
                           dap(wo, ch * 1024, [[8192, 128], [1, 1024]]),
                           waits=[t_cw] if t_cw else [], sem="d_wl")
                t_cw = S.op("gpsimd", lambda e, ch=ch: e.tensor_copy(
                    wob[:, ch * 1024:(ch + 1) * 1024], wof[:, :]),
                    waits=[t_lw], inc="s_gpc")
            tk_xnt = [None, None]
            tk_psw = [None, None]
            tk_st3 = None
            tk_bc3 = None
            tk_c1 = None
            for w4 in range(4):
                t_st = None
                for dc in range(8):
                    wb_ = dc % 2
                    t_lx = dma("sync", xnt[wb_][:, :],
                               dap(xnscr, dc * 2048 + w4 * 512, [[16384, 128], [1, 512]]),
                               waits=[tk_xnt[wb_]], sem=f"d_xn{wb_}")
                    t_pw = None
                    for ac in range(8):
                        t_pw = S.op("tensor", lambda e, dc=dc, ac=ac, w4=w4, wb_=wb_:
                                    e.matmul(
                                        ps_w[wb_][:, :],
                                        wob[:, dc * 1024 + ac * 128:
                                            dc * 1024 + (ac + 1) * 128],
                                        abuf[:, ac * 2048 + w4 * 512:
                                             ac * 2048 + (w4 + 1) * 512],
                                        start=(ac == 0), stop=(ac == 7)),
                                    waits=[t_cw, tk_psw[wb_]],
                                    inc=("s_pe" if ac == 7 else None))
                    r1sl = r1[:, dc * 2048 + w4 * 512: dc * 2048 + (w4 + 1) * 512]
                    t_ev = S.op("vector",
                                lambda e, r=r1sl, dc=dc, wb_=wb_: e.scalar_tensor_tensor(
                                    r, ps_w[wb_][:, :], bo_sb[:, dc:dc + 1],
                                    xnt[wb_][:, :], ALU.add, ALU.add),
                                waits=[t_pw, t_lx], inc="s_dve")
                    tk_xnt[wb_] = t_ev
                    tk_psw[wb_] = t_ev
                    t_c1 = S.op("vector", lambda e, r=r1sl: e.tensor_copy(c1b[:, :], r),
                                waits=[tk_c1, t_ev], inc="s_dve")
                    drain("vector")
                    t_c2 = S.op("vector", lambda e: e.tensor_mul(c2b[:, :], c1b[:, :],
                                                                 c1b[:, :]),
                                waits=[t_c1], inc="s_dve")
                    S.op("tensor", lambda e, dc=dc: e.matmul(
                        ps_s1b[0:1, :], ones_b[:, :], c1b[:, :],
                        start=(dc == 0), stop=(dc == 7)), waits=[t_c1, tk_st3])
                    t_st = S.op("tensor", lambda e, dc=dc: e.matmul(
                        ps_s2b[0:1, :], ones_b[:, :], c2b[:, :],
                        start=(dc == 0), stop=(dc == 7)), waits=[t_c2],
                        inc=("s_pe" if dc == 7 else None))
                    tk_c1 = t_st
                t_sa = S.op("vector", lambda e: e.tensor_scalar(
                    strow[0:1, 0:512], ps_s1b[0:1, :], 1.0 / D, None, ALU.mult),
                    waits=[t_st], inc="s_dve")
                t_sb_ = S.op("vector", lambda e: e.tensor_scalar(
                    strow[0:1, 512:1024], ps_s2b[0:1, :], 1.0 / D, None, ALU.mult),
                    waits=[t_sa], inc="s_dve")
                t_sc_ = S.op("vector", lambda e: e.tensor_mul(
                    stmp[0:1, :], strow[0:1, 0:512], strow[0:1, 0:512]),
                    waits=[t_sb_], inc="s_dve")
                t_v3 = S.op("vector", lambda e: e.tensor_sub(strow[0:1, 1024:1536],
                                                             strow[0:1, 512:1024],
                                                             stmp[0:1, :]),
                            waits=[t_sc_], inc="s_dve")
                tk_st3 = t_v3
                S.op("scalar", lambda e: e.activation(strow[0:1, 1536:2048],
                                                      strow[0:1, 1024:1536], AF.Ln,
                                                      bias=epsv[0:1, 0:1], scale=1.0), waits=[t_v3])
                t_rs3 = S.op("scalar", lambda e: e.activation(strow[0:1, 512:1024],
                                                              strow[0:1, 1536:2048],
                                                              AF.Exp, bias=0.0, scale=-0.5),
                             inc="s_act")
                S.op("tensor", lambda e: e.matmul(ps_mu2[:, :], ones_f[0:1, :],
                                                  strow[0:1, 0:512], start=True, stop=True),
                     waits=[t_rs3, tk_bc3])
                t_bc = S.op("tensor", lambda e: e.matmul(ps_rs2[:, :], ones_f[0:1, :],
                                                         strow[0:1, 512:1024],
                                                         start=True, stop=True), inc="s_pe")
                t_x1 = None
                for dc in range(8):
                    r1sl = r1[:, dc * 2048 + w4 * 512: dc * 2048 + (w4 + 1) * 512]
                    x1sl = x1b[:, dc * 2048 + w4 * 512: dc * 2048 + (w4 + 1) * 512]
                    t_e1 = S.op("vector", lambda e, r=r1sl: e.tensor_tensor(
                        t1f[:, :], r, ps_mu2[:, :], ALU.subtract), waits=[t_bc],
                        inc="s_dve")
                    drain("vector")
                    t_e2 = S.op("vector", lambda e: e.tensor_tensor(t2f[:, :], t1f[:, :],
                                                                    ps_rs2[:, :],
                                                                    ALU.mult),
                                waits=[t_e1], inc="s_dve")
                    drain("vector")
                    S.op("vector", lambda e, r=r1sl, dc=dc: e.tensor_scalar(
                        r, t2f[:, :], g1_sb[:, dc:dc + 1], be1_sb[:, dc:dc + 1],
                        ALU.mult, ALU.add), waits=[t_e2])
                    t_x1 = S.op("vector", lambda e, x=x1sl, dc=dc: e.tensor_scalar(
                        x, t2f[:, :], g1_sb[:, dc:dc + 1], be1_sb[:, dc:dc + 1],
                        ALU.mult, ALU.add), inc="s_dve")
                tk_bc3 = t_x1
            if DEBUG_SCRATCH_OUT:
                dma("gpsimd", dap(x1dbg, 0, [[16384, 128], [1, 16384]]), r1[:, :],
                    waits=[S.cur("s_dve")], sem="d_misc")
            barrier()
        es23.close()
        if STOP_AFTER == "P3":
            return _emit(nc, S, sems)

        es4 = ExitStack()
        with es4:
            h1b = sb(es4, "h1b", (128, 32 * 512), BF16)
            w1f = [sb(es4, f"w1f{i}", (128, 1024), F32) for i in range(2)]
            w1bb = [sb(es4, f"w1bb{i}", (128, 1024), BF16) for i in range(2)]
            w2f = [sb(es4, f"w2f{i}", (128, 1024), F32) for i in range(2)]
            w2bb = [sb(es4, f"w2bb{i}", (128, 1024), BF16) for i in range(2)]
            t1g = sb(es4, "t1g", (128, 512), F32)
            t2g = sb(es4, "t2g", (128, 512), F32)
            c1c = sb(es4, "c1c", (128, 512), BF16)
            c2c = sb(es4, "c2c", (128, 512), BF16)
            ps_h = [psn(es4, f"ps_h{i}", (128, 512)) for i in range(2)]
            ps_y2 = [psn(es4, f"ps_y2{i}", (128, 512)) for i in range(2)]
            ps_s1c = psn(es4, "ps_s1c", (1, 512))
            ps_s2c = psn(es4, "ps_s2c", (1, 512))
            ps_mu3 = psn(es4, "ps_mu3", (128, 512))
            ps_rs3 = psn(es4, "ps_rs3", (128, 512))
            b1_sb = sb(es4, "b1_sb", (128, 32), F32)
            t_lb1 = dma("sync", b1_sb[:, :], b1, sem="d_misc")

            tk_w1 = [None, None]
            tk_w1pe = [None, None]
            tk_w2 = [None, None]
            tk_w2pe = [None, None]
            tk_psh = [None, None]
            tk_psy2 = [None, None]
            tk_h1pe = None
            tk_st4 = None
            tk_bc4 = None
            tk_c1c = None
            tk_r2 = None
            for w4 in range(4):
                t_hr = None
                for ffc in range(32):
                    wi = ffc % 2
                    t_l1w = dma("sync", w1f[wi][:, :],
                                dap(w1, ffc * 1024, [[32768, 128], [1, 1024]]),
                                waits=[tk_w1[wi]], sem=f"d_w4a{wi}")
                    t_c1w = S.op("gpsimd", lambda e, wi=wi: e.tensor_copy(
                        w1bb[wi][:, :], w1f[wi][:, :]),
                        waits=[t_l1w, tk_w1pe[wi]], inc="s_gpc")
                    tk_w1[wi] = t_c1w
                    t_ph = None
                    for dc in range(8):
                        t_ph = S.op("tensor", lambda e, wi=wi, dc=dc, w4=w4: e.matmul(
                            ps_h[wi][:, :], w1bb[wi][:, dc * 128:(dc + 1) * 128],
                            x1b[:, dc * 2048 + w4 * 512: dc * 2048 + (w4 + 1) * 512],
                            start=(dc == 0), stop=(dc == 7)),
                            waits=[t_c1w, tk_psh[wi]], inc=("s_pe" if dc == 7 else None))
                    t_hr = S.op("scalar", lambda e, ffc=ffc, wi=wi: e.activation(
                        h1b[:, ffc * 512:(ffc + 1) * 512], ps_h[wi][:, :], AF.Relu,
                        bias=b1_sb[:, ffc:ffc + 1], scale=1.0),
                        waits=[t_ph, tk_h1pe, t_lb1], inc="s_act")
                    tk_psh[wi] = t_hr
                    tk_w1pe[wi] = t_ph
                t_hlast = t_hr
                t_py = None
                for dc in range(8):
                    yb = dc % 2
                    for qf in range(4):
                        wi = (dc * 4 + qf) % 2
                        t_l2w = dma("sync", w2f[wi][:, :],
                                    dap(w2, dc * 4096 + qf * 1024,
                                        [[32768, 128], [1, 1024]]),
                                    waits=[tk_w2[wi]], sem=f"d_w4b{wi}")
                        t_c2w = S.op("gpsimd", lambda e, wi=wi: e.tensor_copy(
                            w2bb[wi][:, :], w2f[wi][:, :]),
                            waits=[t_l2w, tk_w2pe[wi]], inc="s_gpc")
                        tk_w2[wi] = t_c2w
                        for f8 in range(8):
                            ffc = qf * 8 + f8
                            t_py = S.op("tensor",
                                        lambda e, wi=wi, f8=f8, ffc=ffc, yb=yb: e.matmul(
                                            ps_y2[yb][:, :],
                                            w2bb[wi][:, f8 * 128:(f8 + 1) * 128],
                                            h1b[:, ffc * 512:(ffc + 1) * 512],
                                            start=(ffc == 0), stop=(ffc == 31)),
                                        waits=[t_c2w, t_hlast, tk_psy2[yb]],
                                        inc=("s_pe" if ffc == 31 else None))
                        tk_w2pe[wi] = t_py
                    r1sl = r1[:, dc * 2048 + w4 * 512: dc * 2048 + (w4 + 1) * 512]
                    t_r2 = S.op("vector", lambda e, r=r1sl, dc=dc, yb=yb:
                                e.scalar_tensor_tensor(
                                    r, ps_y2[yb][:, :], b2_sb[:, dc:dc + 1], r,
                                    ALU.add, ALU.add),
                                waits=[t_py], inc="s_dve")
                    tk_psy2[yb] = t_r2
                    tk_r2 = t_r2
                tk_h1pe = t_py
                t_st4 = None
                for dc in range(8):
                    r1sl = r1[:, dc * 2048 + w4 * 512: dc * 2048 + (w4 + 1) * 512]
                    t_c1 = S.op("vector", lambda e, r=r1sl: e.tensor_copy(c1c[:, :], r),
                                waits=[tk_c1c, tk_r2], inc="s_dve")
                    drain("vector")
                    t_c2 = S.op("vector", lambda e: e.tensor_mul(c2c[:, :], c1c[:, :],
                                                                 c1c[:, :]),
                                waits=[t_c1], inc="s_dve")
                    S.op("tensor", lambda e, dc=dc: e.matmul(
                        ps_s1c[0:1, :], ones_b[:, :], c1c[:, :],
                        start=(dc == 0), stop=(dc == 7)), waits=[t_c1, tk_st4])
                    t_st4 = S.op("tensor", lambda e, dc=dc: e.matmul(
                        ps_s2c[0:1, :], ones_b[:, :], c2c[:, :],
                        start=(dc == 0), stop=(dc == 7)), waits=[t_c2],
                        inc=("s_pe" if dc == 7 else None))
                    tk_c1c = t_st4
                t_sa = S.op("vector", lambda e: e.tensor_scalar(
                    strow[0:1, 0:512], ps_s1c[0:1, :], 1.0 / D, None, ALU.mult),
                    waits=[t_st4], inc="s_dve")
                t_sb_ = S.op("vector", lambda e: e.tensor_scalar(
                    strow[0:1, 512:1024], ps_s2c[0:1, :], 1.0 / D, None, ALU.mult),
                    waits=[t_sa], inc="s_dve")
                t_sc_ = S.op("vector", lambda e: e.tensor_mul(
                    stmp[0:1, :], strow[0:1, 0:512], strow[0:1, 0:512]),
                    waits=[t_sb_], inc="s_dve")
                t_v4 = S.op("vector", lambda e: e.tensor_sub(strow[0:1, 1024:1536],
                                                             strow[0:1, 512:1024],
                                                             stmp[0:1, :]),
                            waits=[t_sc_], inc="s_dve")
                tk_st4 = t_v4
                S.op("scalar", lambda e: e.activation(strow[0:1, 1536:2048],
                                                      strow[0:1, 1024:1536], AF.Ln,
                                                      bias=epsv[0:1, 0:1], scale=1.0), waits=[t_v4])
                t_rs4 = S.op("scalar", lambda e: e.activation(strow[0:1, 512:1024],
                                                              strow[0:1, 1536:2048],
                                                              AF.Exp, bias=0.0, scale=-0.5),
                             inc="s_act")
                S.op("tensor", lambda e: e.matmul(ps_mu3[:, :], ones_f[0:1, :],
                                                  strow[0:1, 0:512], start=True, stop=True),
                     waits=[t_rs4, tk_bc4])
                t_bc4 = S.op("tensor", lambda e: e.matmul(ps_rs3[:, :], ones_f[0:1, :],
                                                          strow[0:1, 512:1024],
                                                          start=True, stop=True),
                             inc="s_pe")
                t_z = None
                for dc in range(8):
                    r1sl = r1[:, dc * 2048 + w4 * 512: dc * 2048 + (w4 + 1) * 512]
                    t_f1 = S.op("vector", lambda e, r=r1sl: e.tensor_tensor(
                        t1g[:, :], r, ps_mu3[:, :], ALU.subtract), waits=[t_bc4],
                        inc="s_dve")
                    drain("vector")
                    t_f2 = S.op("vector", lambda e: e.tensor_tensor(t2g[:, :], t1g[:, :],
                                                                    ps_rs3[:, :],
                                                                    ALU.mult),
                                waits=[t_f1], inc="s_dve")
                    drain("vector")
                    t_z = S.op("vector", lambda e, r=r1sl, dc=dc: e.tensor_scalar(
                        r, t2g[:, :], g2_sb[:, dc:dc + 1], be2_sb[:, dc:dc + 1],
                        ALU.mult, ALU.add), waits=[t_f2], inc="s_dve")
                tk_bc4 = t_z
            if DEBUG_SCRATCH_OUT:
                dma("gpsimd", dap(sdbg, 0, [[16384, 128], [1, 16384]]), r1[:, :],
                    waits=[S.cur("s_dve")], sem="d_misc")
            barrier()
        esx1.close()
        if STOP_AFTER == "P4":
            return _emit(nc, S, sems)

        es6 = ExitStack()
        with es6:
            for _ in range(4):
                S.op("vector", lambda e: e.tensor_reduce(padsink[:, 1:2], r1[:, :],
                                                         AX.X, ALU.max))
            S.op("vector", lambda e: e.drain(), inc="s_dve")
            barrier()
            gtl = [sb(es6, f"gtl{i}", (128, 512), F32) for i in range(2)]
            xtl = [sb(es6, f"xtl{i}", (128, 512), F32) for i in range(2)]
            ttl = [sb(es6, f"ttl{i}", (128, 512), F32) for i in range(2)]
            ofull = sb(es6, "ofull", (128, 8 * NOWN), F32)
            tk_g = [None, None]
            tk_x = [None, None]
            t_ot = None
            for dc in range(8):
                for w4 in range(4):
                    bi = (dc * 4 + w4) % 2
                    t_lg = dma("sync", gtl[bi][:, :],
                               dap(gscr, dc * 2048 + w4 * 512, [[16384, 128], [1, 512]]),
                               waits=[tk_g[bi]], sem=f"d_6g{bi}")
                    t_lx = dma("sync", xtl[bi][:, :],
                               dap(xt, dc * 128 * T + w4 * 512, [[T, 128], [1, 512]]),
                               waits=[tk_x[bi]], sem=f"d_6x{bi}")
                    ssl = r1[:, dc * 2048 + w4 * 512: dc * 2048 + (w4 + 1) * 512]
                    t_m6 = S.op("vector", lambda e, bi=bi, s=ssl: e.tensor_tensor(
                        ttl[bi][:, :], gtl[bi][:, :], s, ALU.mult),
                        waits=[t_lg], inc="s_dve")
                    tk_g[bi] = t_m6
                    drain("vector")
                    osl = ofull[:, dc * 2048 + w4 * 512: dc * 2048 + (w4 + 1) * 512]
                    t_ot = S.op("vector", lambda e, bi=bi, o=osl: e.tensor_tensor(
                        o, ttl[bi][:, :], xtl[bi][:, :], ALU.add),
                        waits=[t_lx], inc="s_dve")
                    tk_x[bi] = t_ot
            t_d6 = pad("vector", ofull[:, 15872:16384])
            t_os = dma("gpsimd", dap(out_t, 0, [[NOWN, 128], [128 * NOWN, 8], [1, 2048]]),
                       bass.AP(ofull, 0, [[16384, 128], [2048, 8], [1, 2048]]),
                       waits=[t_d6], sem="d_out")
            if DEBUG_SCRATCH_OUT:
                dma("sync", dap(odbg, 0, [[16384, 128], [1, 16384]]), ofull[:, :],
                    waits=[t_os], sem="d_misc")
            S.op("gpsimd", None, waits=[t_os])
            barrier()
        es34.close()
        return _emit(nc, S, sems)


def _emit(nc, S, sems):
    with nc.Block() as blk:
        for eng_name in ENGS:
            body = [o for o in S.ops if o[0] == eng_name]

            def make_runner(ops):
                def run(e):
                    waited = {}
                    for (_, fn, ws, inc, inck) in ops:
                        for (s, v) in ws:
                            if waited.get(s, 0) < v:
                                e.wait_ge(sems[s], v)
                                waited[s] = v
                        if fn is None:
                            continue
                        inst = fn(e)
                        if inc is not None:
                            inst.then_inc(sems[inc], inck)
                return run

            getattr(blk, eng_name)(make_runner(body))
    return nc


def _prep_shared(inp):
    f = np.float32

    def blkT(W, rows, cols):
        a = W.reshape(rows, 128, cols, 128)
        return np.ascontiguousarray(a.transpose(1, 2, 0, 3).reshape(128, rows * cols * 128),
                                    dtype=f)

    d = {}
    d["wq"] = blkT(np.asarray(inp["L0_attn_Wq"]), 8, 8)
    d["wk"] = blkT(np.asarray(inp["L0_attn_Wk"]), 8, 8)
    d["wv"] = blkT(np.asarray(inp["L0_attn_Wv"]), 8, 8)
    d["wo"] = blkT(np.asarray(inp["L0_attn_Wo"]), 8, 8)
    d["w1"] = blkT(np.asarray(inp["L0_attn_W1"]), 8, 32)
    d["w2"] = blkT(np.asarray(inp["L0_attn_W2"]), 32, 8)
    d["gw"] = np.ascontiguousarray(
        np.stack([blkT(np.asarray(inp["L0_ffn_gate_w"])[e], 8, 16) for e in range(E)]))
    d["uw"] = np.ascontiguousarray(
        np.stack([blkT(np.asarray(inp["L0_ffn_up_w"])[e], 8, 16) for e in range(E)]))
    d["dw"] = np.ascontiguousarray(
        np.stack([blkT(np.asarray(inp["L0_ffn_down_w"])[e], 16, 8) for e in range(E)]))
    d["rw"] = np.ascontiguousarray(
        np.asarray(inp["L0_ffn_router_w"]).T.reshape(8, 128, 8).transpose(1, 0, 2)
        .reshape(128, 64), dtype=f)
    d["dbm"] = np.ascontiguousarray(np.asarray(inp["L0_ffn_down_b"]), dtype=f)

    def vec8(v, nchunk):
        return np.ascontiguousarray(np.asarray(v).reshape(nchunk, 128).T, dtype=f)

    d["bq"] = vec8(inp["L0_attn_bq"], 8)
    d["bk"] = vec8(inp["L0_attn_bk"], 8)
    d["bvr"] = np.ascontiguousarray(np.asarray(inp["L0_attn_bv"]).reshape(1, 1024), dtype=f)
    d["bo"] = vec8(inp["L0_attn_bo"], 8)
    d["b1"] = vec8(inp["L0_attn_b1"], 32)
    d["b2"] = vec8(inp["L0_attn_b2"], 8)
    d["g1"] = vec8(inp["L0_attn_ln1_g"], 8)
    d["be1"] = vec8(inp["L0_attn_ln1_b"], 8)
    d["g2"] = vec8(inp["L0_attn_ln2_g"], 8)
    d["be2"] = vec8(inp["L0_attn_ln2_b"], 8)
    gb = np.asarray(inp["L0_ffn_gate_b"]).reshape(8, 16, 128)
    d["gbi"] = np.ascontiguousarray(gb.transpose(2, 0, 1).reshape(128, 128), dtype=f)
    ub = np.asarray(inp["L0_ffn_up_b"]).reshape(8, 16, 128)
    d["ubi"] = np.ascontiguousarray(ub.transpose(2, 0, 1).reshape(128, 128), dtype=f)
    d["dsc"] = np.asarray(inp["L0_attn_dist_scale"], dtype=f).reshape(1, 1)
    ii = np.arange(128)[:, None]
    jjj = np.arange(1920)[None, :] - 896
    d["w2i"] = np.ascontiguousarray(np.abs(ii - jjj) + jjj, dtype=f)
    return d


def _prep_core(x, c):
    f = np.float32
    b, Toff = c // 4, 2048 * (c % 4)
    d = {}
    xr = np.roll(np.asarray(x[b]), -Toff, axis=0)
    d["xt"] = np.ascontiguousarray(xr.T, dtype=f)
    tv = np.zeros((2, 2048), f)
    for w in range(2):
        pos = np.arange(1024) + 1024 * w
        tv[w, pos] = Toff + pos
    d["tv4"] = tv[[0, 0, 1, 1], :].copy()
    d["msk4"] = np.array([[1], [0], [1], [0]], np.uint8)
    gpos = (Toff + np.arange(T)) % T
    ind = np.zeros((4, T), f)
    p1 = np.zeros((128, 128), f)
    p0 = np.zeros((128, 1024), f)
    for w in range(2):
        t0g = Toff + 1024 * w
        for s in range(64):
            gks = (Toff + 128 * s) % T
            diag = (8 * w <= s < 8 * w + 8)
            lower = (not diag) and (gks < t0g)
            if lower:
                ind[2 * w, 128 * s:128 * (s + 1)] = 1.0
                ind[2 * w + 1, 128 * s:128 * (s + 1)] = 1.0
            gu = gpos[128 * s:128 * (s + 1)].astype(np.int64)
            if diag:
                p1[:, w * 64 + s] = -(Toff + 128 * s)
            elif lower:
                p1[:, w * 64 + s] = gu
            else:
                p1[:, w * 64 + s] = -gu
    d["ind4"] = ind.astype(ml_dtypes.bfloat16)
    for h in range(H):
        for s in range(64):
            gu = gpos[128 * s:128 * (s + 1)]
            m = gu < h
            if m.any():
                p0[m, h * 64 + s] = NEG
    d["p1d"] = p1
    d["p0d"] = p0
    return d


def kernel(**inputs):
    x = np.asarray(inputs["L0_attn_tokens"], dtype=np.float32)
    shared = _prep_shared(inputs)
    nc = build_nc()
    in_maps = []
    for c in range(8):
        m = dict(shared)
        m.update(_prep_core(x, c))
        in_maps.append(m)
    res = run_bass_kernel_spmd(nc, in_maps, core_ids=list(range(8)))
    out = np.zeros((B, T, D), np.float32)
    for c in range(8):
        b, Toff = c // 4, 2048 * (c % 4)
        out[b, Toff:Toff + 2048, :] = res.results[c]["out_t"].T
    return out
